# version 33
# baseline (speedup 1.0000x reference)
import numpy as np
import ml_dtypes
import concourse.bass as bass
import concourse.mybir as mybir
from concourse.bass_utils import run_bass_kernel_spmd
from contextlib import ExitStack

F32 = mybir.dt.float32
BF16 = mybir.dt.bfloat16
U8 = mybir.dt.uint8
AF = mybir.ActivationFunctionType
ALU = mybir.AluOpType

NT = 8192
D = 4096
TS = 512
SC = 32
RC = 64
EPS = 1e-5
G_SIZES = [6, 6, 6, 6, 6, 2]
G_BASE = [0, 6, 12, 18, 24, 30]
CCOL = {"sum": 0, "sumsq": 4, "mean": 8, "negmean": 12, "ex2e": 16, "msq": 20,
        "varpe": 24, "srt": 28, "rstd": 32, "negmb": 36}


def build_program(debug=False, delay_dve=0):
    nc = bass.Bass("TRN2", target_bir_lowering=False)

    x_in = nc.dram_tensor("x_in", [NT, D], F32, kind="ExternalInput").ap()
    xres_in = nc.dram_tensor("xres_in", [NT, TS], F32, kind="ExternalInput").ap()
    w1_in = nc.dram_tensor("w1_in", [D, D], F32, kind="ExternalInput").ap()
    w2_in = nc.dram_tensor("w2_in", [TS, D], F32, kind="ExternalInput").ap()
    b2_in = nc.dram_tensor("b2_in", [1, TS], F32, kind="ExternalInput").ap()
    sm_in = nc.dram_tensor("sm_in", [D, TS], F32, kind="ExternalInput").ap()
    W_in = nc.dram_tensor("W_in", [TS, D], F32, kind="ExternalInput").ap()
    fm_in = nc.dram_tensor("fm_in", [TS, D], U8, kind="ExternalInput").ap()
    id_in = nc.dram_tensor("id_in", [128, 128], F32, kind="ExternalInput").ap()
    out_t = nc.dram_tensor("out_t", [NT, TS], F32, kind="ExternalOutput").ap()
    if debug:
        dbg_w2T = nc.dram_tensor("dbg_w2T", [128, SC * TS], BF16, kind="ExternalOutput").ap()
        dbg_Ab = nc.dram_tensor("dbg_Ab", [128, SC * TS], BF16, kind="ExternalOutput").ap()
        dbg_WmT = nc.dram_tensor("dbg_WmT", [128, SC * TS], BF16, kind="ExternalOutput").ap()
        dbg_xn = nc.dram_tensor("dbg_xn", [128, D], BF16, kind="ExternalOutput").ap()
        dbg_xnT = nc.dram_tensor("dbg_xnT", [128, D], BF16, kind="ExternalOutput").ap()
        dbg_stt = nc.dram_tensor("dbg_stt", [128, 40 * 3], F32, kind="ExternalOutput").ap()

    ops = {e: [] for e in ["tensor", "vector", "scalar", "gpsimd", "sync"]}
    cnt = {}
    MS = {}
    sems = {}

    def W(eng, key):
        sem, val = MS[key]
        ops[eng].append(("wait", sem, val))

    def O(eng, fn, inc=None, n=1):
        ops[eng].append(("op", fn, None, None))
        if inc is not None:
            cnt[inc] += n
            return cnt[inc]
        return None

    def M(key, semname):
        MS[key] = (sems[semname], cnt[semname])

    with ExitStack() as es:
        E = es.enter_context
        w2T = E(nc.sbuf_tensor("w2T", [128, SC * TS], BF16))
        Ab = E(nc.sbuf_tensor("Ab", [128, SC * TS], BF16))
        WmT = E(nc.sbuf_tensor("WmT", [128, SC * TS], BF16))
        pb = [E(nc.sbuf_tensor(f"pb{i}", [128, 1024], BF16)) for i in range(6)]
        fmu = [E(nc.sbuf_tensor(f"fmu{i}", [128, 1024], U8)) for i in range(2)]
        fmb = E(nc.sbuf_tensor("fmb", [128, 1024], BF16))
        w1b = E(nc.sbuf_tensor("w1b", [128, 16 * 384], BF16))
        smt = [E(nc.sbuf_tensor(f"smt{i}", [128, 512], F32)) for i in range(3)]
        xb = [E(nc.sbuf_tensor(f"xb{i}", [128, D], BF16)) for i in range(3)]
        xn = [E(nc.sbuf_tensor(f"xn{i}", [128, D], BF16)) for i in range(2)]
        xnT = [E(nc.sbuf_tensor(f"xnT{i}", [128, D], BF16)) for i in range(2)]
        xrs = [E(nc.sbuf_tensor(f"xrs{i}", [128, TS], F32)) for i in range(4)]
        gs = [E(nc.sbuf_tensor(f"gs{i}", [128, TS], F32)) for i in range(2)]
        os_ = [E(nc.sbuf_tensor(f"os{i}", [128, TS], F32)) for i in range(2)]
        stt = [E(nc.sbuf_tensor(f"stt{i}", [128, 40], F32)) for i in range(3)]
        identb = E(nc.sbuf_tensor("identb", [128, 128], BF16))
        b2row = E(nc.sbuf_tensor("b2row", [1, TS], BF16))
        b2B = E(nc.sbuf_tensor("b2B", [128, TS], F32))
        ones1 = E(nc.sbuf_tensor("ones1", [1, 128], BF16))
        pA = [E(nc.psum_tensor(f"pA{i}", [128, 512], F32)) for i in range(6)]
        pX = [E(nc.psum_tensor(f"pX{i}", [128, 1024], BF16)) for i in range(2)]
        semnames = (["pe", "dv", "ac", "id", "b2"]
                    + [f"pb{i}" for i in range(6)]
                    + [f"fmu{i}" for i in range(2)]
                    + [f"w1p{i}" for i in range(8)]
                    + [f"smt{i}" for i in range(3)]
                    + [f"xb{i}" for i in range(3)]
                    + [f"xrs{i}" for i in range(4)]
                    + [f"st{i}" for i in range(2)])
        for nm in semnames:
            sems[nm] = E(nc.semaphore("sm_" + nm))
            cnt[nm] = 0
        s_pe, s_dv, s_ac = sems["pe"], sems["dv"], sems["ac"]

        _nums = [sems[nm].num for nm in semnames]
        assert max(_nums) - min(_nums) + 1 == len(_nums), "sems not contiguous"
        nc.gpsimd.sem_clear(range(min(_nums), max(_nums) + 1))
        nc.gpsimd.dma_start(out=identb[:, :], in_=id_in[:, :]).then_inc(sems["id"], 16)
        cnt["id"] = 16
        MS["id_ld"] = (sems["id"], 16)
        for _u in range(6):
            _qh, _tc = divmod(_u, 4)
            nc.gpsimd.dma_start(
                out=pb[_u][:, :],
                in_=w2_in[_tc * 128:(_tc + 1) * 128,
                          _qh * 1024:(_qh + 1) * 1024]).then_inc(sems[f"pb{_u}"], 16)
            cnt[f"pb{_u}"] = 16
            MS[("w2_ld", _u)] = (sems[f"pb{_u}"], 16)
        nc.gpsimd.dma_start(out=b2row[0:1, :],
                            in_=b2_in[0:1, :]).then_inc(sems["b2"], 16)
        cnt["b2"] = 16
        MS["b2_ld"] = (sems["b2"], 16)
        nc.all_engine_barrier()

        w2T32 = w2T.ap().rearrange("p (c t) -> p c t", t=TS)
        Ab32 = Ab.ap().rearrange("p (c t) -> p c t", t=TS)
        WmT32 = WmT.ap().rearrange("p (c t) -> p c t", t=TS)


        O("vector", lambda v: v.memset(ones1[0:1, :], 1.0).then_inc(s_dv, 1),
          inc="dv")
        M("ones", "dv")
        for _ in range(delay_dve):
            O("vector", lambda v: v.memset(xb[0][:, :], 0.0))

        fill_id = [0]

        def emit_fill(src_tile, col0):
            f = fill_id[0]
            bank = f % 2
            if f >= 2:
                W("tensor", ("evac", f - 2))
            for j in range(8):
                c = col0 + j * 128
                if j == 7:
                    O("tensor", lambda t, s=src_tile, c=c, b=bank, j=j:
                      t.transpose(pX[b][:, j * 128:(j + 1) * 128],
                                  s[:, c:c + 128], identb[:, :]).then_inc(s_pe, 1),
                      inc="pe")
                else:
                    O("tensor", lambda t, s=src_tile, c=c, b=bank, j=j:
                      t.transpose(pX[b][:, j * 128:(j + 1) * 128],
                                  s[:, c:c + 128], identb[:, :]))
            M(("fill", f), "pe")
            fill_id[0] += 1
            return f, bank

        def emit_evac(f, bank, dst_ap):
            W("vector", ("fill", f))
            src = pX[bank].ap().rearrange("p (b k) -> p b k", k=128)
            O("vector", lambda v, s=src, d=dst_ap:
              v.tensor_copy(d, s).then_inc(s_dv, 1), inc="dv")
            M(("evac", f), "dv")

        def emit_w2_unit(u):
            qh, tc = divmod(u, 4)
            slot = f"pb{u % 6}"
            if u >= 6:
                W("gpsimd", ("fill_unit", u - 6))
                O("gpsimd", lambda g, tc=tc, qh=qh, u=u, s=sems[slot]:
                  g.dma_start(out=pb[u % 6][:, :],
                              in_=w2_in[tc * 128:(tc + 1) * 128,
                                        qh * 1024:(qh + 1) * 1024])
                  .then_inc(s, 16), inc=slot, n=16)
                M(("w2_ld", u), slot)
            if True:
                pass
            if u == 0:
                W("tensor", "id_ld")
            W("tensor", ("w2_ld", u))
            f, bank = emit_fill(pb[u % 6], 0)
            emit_evac(f, bank, w2T32[:, qh * 8: qh * 8 + 8, tc * 128:(tc + 1) * 128])
            M(("fill_unit", u), "pe")
            if tc == 3:
                M(("w2T_rdy", qh), "dv")

        for u in range(4):
            emit_w2_unit(u)

        W("tensor", "b2_ld")
        W("tensor", "ones")
        O("tensor", lambda t:
          t.matmul(pA[0][:, :], ones1[0:1, :], b2row[0:1, :],
                   start=True, stop=True).then_inc(s_pe, 1), inc="pe")
        M("b2B_mm", "pe")
        W("vector", "b2B_mm")
        O("vector", lambda v:
          v.tensor_copy(b2B[:, :], pA[0][:, :]).then_inc(s_dv, 1), inc="dv")
        M("b2B_ev", "dv")

        def emit_wm_fmu(u):
            oc, qh = divmod(u, 4)
            fslot = f"fmu{u % 2}"
            if u >= 2:
                W("sync", ("wm_mult", u - 2))
            O("sync", lambda sy, oc=oc, qh=qh, u=u, s=sems[fslot]:
              sy.dma_start(out=fmu[u % 2][:, :],
                           in_=fm_in[oc * 128:(oc + 1) * 128,
                                     qh * 1024:(qh + 1) * 1024])
              .then_inc(s, 16), inc=fslot, n=16)
            M(("fm_ld", u), fslot)

        def emit_wm_dve(u):
            oc, qh = divmod(u, 4)
            l = 16 + u
            wslot = f"pb{l % 6}"
            fslot = f"fmu{u % 2}"
            W("gpsimd", ("fill_unit", l - 6))
            O("gpsimd", lambda g, oc=oc, qh=qh, l=l, s=sems[wslot]:
              g.dma_start(out=pb[l % 6][:, :],
                          in_=W_in[oc * 128:(oc + 1) * 128,
                                   qh * 1024:(qh + 1) * 1024])
              .then_inc(s, 16), inc=wslot, n=16)
            M(("W_ld", u), wslot)
            W("vector", ("fm_ld", u))
            W("vector", ("W_ld", u))
            W("vector", ("fill_unit", 16 + u - 6))
            if u >= 1:
                W("vector", ("wm_mult", u - 1))
            O("vector", lambda v, u=u:
              v.tensor_copy(fmb[:, :], fmu[u % 2][:, :]).then_inc(s_dv, 1),
              inc="dv")
            M(("fm_cast", u), "dv")
            W("vector", ("fm_cast", u))
            O("vector", lambda v, u=u, l=16 + u:
              v.tensor_tensor(pb[l % 6][:, :], pb[l % 6][:, :], fmb[:, :],
                              ALU.mult).then_inc(s_dv, 1), inc="dv")
            M(("wm_mult", u), "dv")

        def emit_wm_pe(u):
            oc, qh = divmod(u, 4)
            l = 16 + u
            W("tensor", ("wm_mult", u))
            f, bank = emit_fill(pb[l % 6], 0)
            emit_evac(f, bank, WmT32[:, qh * 8: qh * 8 + 8, oc * 128:(oc + 1) * 128])
            M(("fill_unit", l), "pe")
            if u == 15:
                M("wmt_all_evac", "dv")

        def emit_stage_a_load(c_ld):
                xslot = f"xb{c_ld % 3}"
                if c_ld >= 3:
                    W("gpsimd", ("red", c_ld - 3))
                    W("gpsimd", ("norm", c_ld - 3))
                O("gpsimd", lambda g, c=c_ld, s=sems[xslot]:
                  g.dma_start(out=xb[c % 3][:, :],
                              in_=x_in[c * 128:(c + 1) * 128, :])
                  .then_inc(s, 16), inc=xslot, n=16)
                M(("xb_ld", c_ld), xslot)
                rslot = f"xrs{c_ld % 4}"
                if c_ld >= 4:
                    W("sync", ("epi", c_ld - 4))
                O("sync", lambda sy, c=c_ld, s=sems[rslot]:
                  sy.dma_start(out=xrs[c % 4][:, :],
                               in_=xres_in[c * 128:(c + 1) * 128, :])
                  .then_inc(s, 16), inc=rslot, n=16)
                M(("xr_ld", c_ld), rslot)

        def emit_stage_a_calc(c_ld):
                st_ = stt[c_ld % 3]

                def col(nm, s=st_):
                    return s[:, CCOL[nm]:CCOL[nm] + 1]

                W("scalar", ("xb_ld", c_ld))
                if c_ld >= 2:
                    W("scalar", ("xp", c_ld - 2))
                O("scalar", lambda s, c=c_ld, a=col("sumsq"):
                  s.activation(xn[c % 2][:, :], xb[c % 3][:, :], AF.Square,
                               accum_out=a).then_inc(s_ac, 1), inc="ac")
                M(("sq", c_ld), "ac")
                W("vector", ("xb_ld", c_ld))
                O("vector", lambda v, c=c_ld, a=col("sum"):
                  v.reduce_sum(a, xb[c % 3][:, :],
                               mybir.AxisListType.X).then_inc(s_dv, 1), inc="dv")
                M(("red", c_ld), "dv")
                W("vector", ("red", c_ld))
                W("vector", ("sq", c_ld))
                O("vector", lambda v, a=col("mean"), b=col("sum"):
                  v.tensor_scalar(a, b, 1.0 / D, None, ALU.mult).then_inc(s_dv, 1),
                  inc="dv")
                O("vector", lambda v, a=col("negmean"), b=col("sum"):
                  v.tensor_scalar(a, b, -1.0 / D, None, ALU.mult).then_inc(s_dv, 1),
                  inc="dv")
                O("vector", lambda v, a=col("ex2e"), b=col("sumsq"):
                  v.tensor_scalar(a, b, 1.0 / D, EPS, ALU.mult,
                                  ALU.add).then_inc(s_dv, 1), inc="dv")
                M(("chain", c_ld), "dv")
                W("scalar", ("chain", c_ld))
                O("scalar", lambda s, a=col("msq"), b=col("mean"):
                  s.activation(a, b, AF.Identity, scale=b).then_inc(s_ac, 1),
                  inc="ac")
                M(("msq", c_ld), "ac")
                W("vector", ("msq", c_ld))
                O("vector", lambda v, a=col("varpe"), b=col("ex2e"), d2=col("msq"):
                  v.tensor_tensor(a, b, d2, ALU.subtract).then_inc(s_dv, 1),
                  inc="dv")
                M(("varpe", c_ld), "dv")
                W("scalar", ("varpe", c_ld))
                O("scalar", lambda s, a=col("srt"), b=col("varpe"):
                  s.activation(a, b, AF.Sqrt).then_inc(s_ac, 1), inc="ac")
                M(("sqrt", c_ld), "ac")
                W("vector", ("sqrt", c_ld))
                O("vector", lambda v, a=col("rstd"), b=col("srt"):
                  v.reciprocal(a, b).then_inc(s_dv, 1), inc="dv")
                M(("rstd", c_ld), "dv")
                W("scalar", ("rstd", c_ld))
                O("scalar", lambda s, a=col("negmb"), b=col("negmean"), r=col("rstd"):
                  s.activation(a, b, AF.Identity, scale=r).then_inc(s_ac, 1),
                  inc="ac")
                M(("negmb", c_ld), "ac")
                W("scalar", ("negmb", c_ld))
                O("scalar", lambda s, c=c_ld, a=col("negmb"), r=col("rstd"):
                  s.activation(xn[c % 2][:, :], xb[c % 3][:, :], AF.Identity,
                               bias=a, scale=r).then_inc(s_ac, 1), inc="ac")
                M(("norm", c_ld), "ac")


        def emit_xp(c_xp):
                W("tensor", ("norm", c_xp))
                for q in range(4):
                    f, bank = emit_fill(xn[c_xp % 2], q * 8 * 128)
                    if c_xp >= 2:
                        W("vector", ("mm2", c_xp - 2))
                    dst = xnT[c_xp % 2].ap().rearrange(
                        "p (b k) -> p b k", k=128)[:, q * 8:(q + 1) * 8, :]
                    emit_evac(f, bank, dst)
                    MS[("evac_xp", (c_xp, q))] = MS[("evac", f)]
                M(("xp", c_xp), "pe")


        G2 = [3] * 10 + [2]
        G2B = [3 * i for i in range(11)]
        smi = 0
        for g in range(11):
            sb, sz = G2B[g], G2[g]
            bank0 = (g % 2) * 3
            for d in range(32):
                n = g * 32 + d
                if d % 2 == 0:
                    wslot = f"w1p{(n // 2) % 8}"
                    if n >= 16:
                        pg, pd = divmod(n - 16 + 1, 32)
                        W("gpsimd", ("mm1", (pg, pd)))
                    src_ap = w1_in[d * 128:(d + 2) * 128,
                                   sb * 128:(sb + sz) * 128].rearrange(
                        "(a p) c -> p a c", p=128)
                    dst_ap = w1b.ap().rearrange(
                        "p (s c) -> p s c", c=384)[:, n % 16:n % 16 + 2,
                                                   0:sz * 128]
                    O("gpsimd", lambda g_, sa=src_ap, da=dst_ap, s=sems[wslot]:
                      g_.dma_start(out=da, in_=sa).then_inc(s, 16),
                      inc=wslot, n=16)
                    M(("w1_ld", (g, d)), wslot)
                    MS[("w1_ld", (g, d + 1))] = MS[("w1_ld", (g, d))]
                if g == 0 and d % 8 == 0 and d > 0:
                    for u in range(d // 8 * 4, d // 8 * 4 + 4):
                        emit_w2_unit(u)
                W("tensor", ("w1_ld", (g, d)))
                if g == 0 and d % 8 == 0:
                    W("tensor", ("w2T_rdy", d // 8))
                for j in range(sz):
                    if d == 0 and g >= 2:
                        W("tensor", ("evA_bank", (g - 2, j)))
                    if d == 0 and g == 0 and j == 0:
                        W("tensor", "b2B_ev")
                    if j == sz - 1:
                        O("tensor", lambda t, j=j, d=d, n=n, b=bank0:
                          t.matmul(pA[b + j][:, :],
                                   w1b.ap()[:, (n % 16) * 384 + j * 128:
                                            (n % 16) * 384 + (j + 1) * 128],
                                   w2T32[:, d, :], start=(d == 0),
                                   stop=(d == 31)).then_inc(s_pe, 1), inc="pe")
                    else:
                        O("tensor", lambda t, j=j, d=d, n=n, b=bank0:
                          t.matmul(pA[b + j][:, :],
                                   w1b.ap()[:, (n % 16) * 384 + j * 128:
                                            (n % 16) * 384 + (j + 1) * 128],
                                   w2T32[:, d, :], start=(d == 0), stop=(d == 31)))
                M(("mm1", (g, d)), "pe")
            if g == 0:
                emit_wm_fmu(0)
                emit_wm_fmu(1)
                emit_wm_dve(0)
                emit_wm_dve(1)
            if g + 1 <= 7:
                emit_wm_fmu(2 * (g + 1))
                emit_wm_fmu(2 * (g + 1) + 1)
            W("vector", ("mm1", (g, 31)))
            for j in range(sz):
                sslot = f"smt{smi % 3}"
                if smi >= 3:
                    W("sync", ("evA", smi - 3))
                O("sync", lambda sy, sb=sb, j=j, smi=smi, s=sems[sslot]:
                  sy.dma_start(out=smt[smi % 3][:, :],
                               in_=sm_in[(sb + j) * 128:(sb + j + 1) * 128, :])
                  .then_inc(s, 16), inc=sslot, n=16)
                M(("sm_ld", smi), sslot)
                W("vector", ("sm_ld", smi))
                O("vector", lambda v, j=j, sb=sb, smi=smi, b=bank0:
                  v.tensor_tensor(Ab32[:, sb + j, :], pA[b + j][:, :],
                                  smt[smi % 3][:, :],
                                  ALU.mult).then_inc(s_dv, 1), inc="dv")
                M(("evA", smi), "dv")
                M(("evA_bank", (g, j)), "dv")
                smi += 1
            if g <= 7:
                emit_wm_pe(2 * g)
                emit_wm_pe(2 * g + 1)
            if g + 1 <= 7:
                emit_wm_dve(2 * (g + 1))
                emit_wm_dve(2 * (g + 1) + 1)
            if g in (4, 5, 6, 7):
                cw, half = divmod(g - 4, 2)
                xslot = f"xb{cw % 3}"
                O("gpsimd", lambda g_, c=cw, h=half, s=sems[xslot]:
                  g_.dma_start(out=xb[c % 3][:, h * 2048:(h + 1) * 2048],
                               in_=x_in[c * 128:(c + 1) * 128,
                                        h * 2048:(h + 1) * 2048])
                  .then_inc(s, 16), inc=xslot, n=16)
                if half == 1:
                    M(("xb_ld", cw), xslot)
                    rslot = f"xrs{cw % 4}"
                    O("sync", lambda sy, c=cw, s=sems[rslot]:
                      sy.dma_start(out=xrs[c % 4][:, :],
                                   in_=xres_in[c * 128:(c + 1) * 128, :])
                      .then_inc(s, 16), inc=rslot, n=16)
                    M(("xr_ld", cw), rslot)
            if g == 8:
                emit_stage_a_calc(0)
                emit_stage_a_calc(1)
            if g == 9:
                emit_xp(0)
        M("p1_all_evac", "dv")

        P2PAIRS = [(2, 3), (4, 5), (0, 1)]

        for k in range(RC + 3):
            c_ld, c_xp, c_mm, c_ep = k, k - 1, k - 2, k - 3
            if 2 <= c_ld < RC:
                emit_stage_a_load(c_ld)
                emit_stage_a_calc(c_ld)
            if 0 <= c_mm < RC:
                pY = pA[P2PAIRS[c_mm % 3][0]]
                pF = pA[P2PAIRS[c_mm % 3][1]]
                if c_mm >= 3:
                    W("tensor", ("gelu", c_mm - 3))
                    W("tensor", ("epi", c_mm - 3))
                elif c_mm == 0:
                    W("tensor", "p1_all_evac")
                    W("tensor", "wmt_all_evac")
                elif c_mm == 1:
                    W("tensor", ("evA_bank", (9, 1)))
                    W("tensor", ("evA_bank", (9, 2)))
                else:
                    W("tensor", ("evA_bank", (10, 0)))
                    W("tensor", ("evA_bank", (10, 1)))
                do_xp = 1 <= c_xp < RC

                def emit_fill_q(q, c_xp=c_xp):
                    f, bank = emit_fill(xn[c_xp % 2], q * 8 * 128)
                    if c_xp >= 2:
                        W("vector", ("mm2", c_xp - 2))
                    dst = xnT[c_xp % 2].ap().rearrange(
                        "p (b k) -> p b k", k=128)[:, q * 8:(q + 1) * 8, :]
                    emit_evac(f, bank, dst)
                    MS[("evac_xp", (c_xp, q))] = MS[("evac", f)]

                for sc in range(SC):
                    if sc % 8 == 0:
                        W("tensor", ("evac_xp", (c_mm, sc // 8)))
                    O("tensor", lambda t, c=c_mm, sc=sc, pY=pY:
                      t.matmul(pY[:, :], xnT[c % 2][:, sc * 128:(sc + 1) * 128],
                               Ab32[:, sc, :], start=(sc == 0), stop=(sc == SC - 1)))
                    if sc == SC - 1:
                        O("tensor", lambda t, c=c_mm, sc=sc, pF=pF:
                          t.matmul(pF[:, :], xnT[c % 2][:, sc * 128:(sc + 1) * 128],
                                   WmT32[:, sc, :], start=(sc == 0),
                                   stop=True).then_inc(s_pe, 1), inc="pe")
                    else:
                        O("tensor", lambda t, c=c_mm, sc=sc, pF=pF:
                          t.matmul(pF[:, :], xnT[c % 2][:, sc * 128:(sc + 1) * 128],
                                   WmT32[:, sc, :], start=(sc == 0), stop=False))
                    if do_xp and sc == 7:
                        W("tensor", ("norm", c_xp))
                        emit_fill_q(0)
                        emit_fill_q(1)
                M(("mm2", c_mm), "pe")
                if do_xp:
                    emit_fill_q(2)
                    emit_fill_q(3)
                    M(("xp", c_xp), "pe")
            elif 1 <= c_xp < RC:
                emit_xp(c_xp)
            if 0 <= c_ep < RC:
                pY = pA[P2PAIRS[c_ep % 3][0]]
                pF = pA[P2PAIRS[c_ep % 3][1]]
                W("vector", ("mm2", c_ep))
                if c_ep >= 2:
                    W("vector", ("epi", c_ep - 2))
                O("vector", lambda v, c=c_ep, pY=pY:
                  v.tensor_tensor(gs[c % 2][:, :], pY[:, :], b2B[:, :],
                                  ALU.add).then_inc(s_dv, 1), inc="dv")
                M(("yb", c_ep), "dv")
                W("scalar", ("yb", c_ep))
                O("scalar", lambda s, c=c_ep:
                  s.activation(gs[c % 2][:, :], gs[c % 2][:, :],
                               AF.Gelu).then_inc(s_ac, 1), inc="ac")
                M(("gelu", c_ep), "ac")
                W("vector", ("gelu", c_ep))
                W("vector", ("xr_ld", c_ep))
                if c_ep >= 2:
                    W("vector", ("store", c_ep - 2))
                O("vector", lambda v, c=c_ep:
                  v.tensor_tensor(os_[c % 2][:, :], gs[c % 2][:, :],
                                  xrs[c % 4][:, :], ALU.add).then_inc(s_dv, 1),
                  inc="dv")
                M(("epi_a", c_ep), "dv")
                W("vector", ("epi_a", c_ep))
                O("vector", lambda v, c=c_ep, pF=pF:
                  v.tensor_tensor(os_[c % 2][:, :], os_[c % 2][:, :],
                                  pF[:, :], ALU.add).then_inc(s_dv, 1), inc="dv")
                M(("epi", c_ep), "dv")
                stslot = f"st{c_ep % 2}"
                W("sync", ("epi", c_ep))
                O("sync", lambda sy, c=c_ep, s=sems[stslot]:
                  sy.dma_start(out=out_t[c * 128:(c + 1) * 128, :],
                               in_=os_[c % 2][:, :]).then_inc(s, 16),
                  inc=stslot, n=16)
                M(("store", c_ep), stslot)

        ops["sync"].append(("wait", sems["st0"], cnt["st0"]))
        ops["sync"].append(("wait", sems["st1"], cnt["st1"]))

        if debug:
            def _dump(sy):
                sy.dma_start(out=dbg_w2T[:, :], in_=w2T.ap()[:, :]).then_inc(sems["st0"], 16)
                sy.dma_start(out=dbg_Ab[:, :], in_=Ab.ap()[:, :]).then_inc(sems["st0"], 16)
                sy.dma_start(out=dbg_WmT[:, :], in_=WmT.ap()[:, :]).then_inc(sems["st0"], 16)
                sy.dma_start(out=dbg_xn[:, :], in_=xn[(RC - 1) % 2].ap()[:, :]).then_inc(sems["st0"], 16)
                sy.dma_start(out=dbg_xnT[:, :], in_=xnT[(RC - 1) % 2].ap()[:, :]).then_inc(sems["st0"], 16)
                for i in range(3):
                    sy.dma_start(out=dbg_stt[:, i * 40:(i + 1) * 40],
                                 in_=stt[i].ap()[:, :]).then_inc(sems["st0"], 16)
            O("sync", _dump, inc="st0", n=16 * 8)
            ops["sync"].append(("wait", sems["st0"], cnt["st0"]))

        with nc.Block() as block:

            @block.tensor
            def _(t):
                for item in ops["tensor"]:
                    if item[0] == "wait":
                        t.wait_ge(item[1], item[2])
                    else:
                        item[1](t)

            @block.vector
            def _(v):
                for item in ops["vector"]:
                    if item[0] == "wait":
                        v.wait_ge(item[1], item[2])
                    else:
                        item[1](v)

            @block.scalar
            def _(s):
                for item in ops["scalar"]:
                    if item[0] == "wait":
                        s.wait_ge(item[1], item[2])
                    else:
                        item[1](s)

            @block.gpsimd
            def _(g):
                for item in ops["gpsimd"]:
                    if item[0] == "wait":
                        g.wait_ge(item[1], item[2])
                    else:
                        item[1](g)

            @block.sync
            def _(sy):
                for item in ops["sync"]:
                    if item[0] == "wait":
                        sy.wait_ge(item[1], item[2])
                    else:
                        item[1](sy)

    return nc


def kernel(L0_attn_x, L0_attn_w1, L0_attn_w2, L0_attn_b2, L0_attn_sparse_mask,
           L0_ffn_W, L0_ffn_mask):
    x = np.ascontiguousarray(np.asarray(L0_attn_x, dtype=np.float32)).reshape(NT, D)
    w1 = np.ascontiguousarray(np.asarray(L0_attn_w1, dtype=np.float32))
    w2 = np.ascontiguousarray(np.asarray(L0_attn_w2, dtype=np.float32))
    b2 = np.ascontiguousarray(np.asarray(L0_attn_b2, dtype=np.float32))
    sm = np.ascontiguousarray(np.asarray(L0_attn_sparse_mask, dtype=np.float32))
    Wf = np.ascontiguousarray(np.asarray(L0_ffn_W, dtype=np.float32))
    fm = np.ascontiguousarray(np.asarray(L0_ffn_mask)).astype(np.uint8)
    ident = np.eye(128, dtype=np.float32)

    nc = build_program()
    in_maps = []
    for c in range(8):
        cs = c * TS
        in_maps.append({
            "x_in": x,
            "xres_in": np.ascontiguousarray(x[:, cs:cs + TS]),
            "w1_in": w1,
            "w2_in": np.ascontiguousarray(w2[cs:cs + TS, :]),
            "b2_in": np.ascontiguousarray(b2[cs:cs + TS]).reshape(1, TS),
            "sm_in": np.ascontiguousarray(sm[:, cs:cs + TS]),
            "W_in": np.ascontiguousarray(Wf[cs:cs + TS, :]),
            "fm_in": np.ascontiguousarray(fm[cs:cs + TS, :]),
            "id_in": ident,
        })
    res = run_bass_kernel_spmd(nc, in_maps, core_ids=list(range(8)))
    out = np.concatenate([np.asarray(r["out_t"], dtype=np.float32)
                          for r in res.results], axis=1)
    return out.reshape(2, 4096, 4096)


# version 34
# speedup vs baseline: 1.1770x; 1.1770x over previous
import numpy as np
import ml_dtypes
import concourse.bass as bass
import concourse.mybir as mybir
from concourse.bass_utils import run_bass_kernel_spmd
from contextlib import ExitStack

F32 = mybir.dt.float32
BF16 = mybir.dt.bfloat16
U8 = mybir.dt.uint8
AF = mybir.ActivationFunctionType
ALU = mybir.AluOpType

NT = 8192
D = 4096
TS = 512
SC = 32
RC = 64
EPS = 1e-5
G_SIZES = [6, 6, 6, 6, 6, 2]
G_BASE = [0, 6, 12, 18, 24, 30]
CCOL = {"sum": 0, "sumsq": 4, "mean": 8, "negmean": 12, "ex2e": 16, "msq": 20,
        "varpe": 24, "srt": 28, "rstd": 32, "negmb": 36}


def build_program(debug=False, delay_dve=0):
    nc = bass.Bass("TRN2", target_bir_lowering=False)

    x_in = nc.dram_tensor("x_in", [NT, D], F32, kind="ExternalInput").ap()
    xres_in = nc.dram_tensor("xres_in", [NT, TS], F32, kind="ExternalInput").ap()
    w1_in = nc.dram_tensor("w1_in", [D, D], F32, kind="ExternalInput").ap()
    w2_in = nc.dram_tensor("w2_in", [TS, D], F32, kind="ExternalInput").ap()
    b2_in = nc.dram_tensor("b2_in", [1, TS], F32, kind="ExternalInput").ap()
    sm_in = nc.dram_tensor("sm_in", [D, TS], F32, kind="ExternalInput").ap()
    W_in = nc.dram_tensor("W_in", [TS, D], F32, kind="ExternalInput").ap()
    fm_in = nc.dram_tensor("fm_in", [TS, D], U8, kind="ExternalInput").ap()
    id_in = nc.dram_tensor("id_in", [128, 128], F32, kind="ExternalInput").ap()
    out_t = nc.dram_tensor("out_t", [NT, TS], F32, kind="ExternalOutput").ap()
    if debug:
        dbg_w2T = nc.dram_tensor("dbg_w2T", [128, SC * TS], BF16, kind="ExternalOutput").ap()
        dbg_Ab = nc.dram_tensor("dbg_Ab", [128, SC * TS], BF16, kind="ExternalOutput").ap()
        dbg_WmT = nc.dram_tensor("dbg_WmT", [128, SC * TS], BF16, kind="ExternalOutput").ap()
        dbg_xn = nc.dram_tensor("dbg_xn", [128, D], BF16, kind="ExternalOutput").ap()
        dbg_xnT = nc.dram_tensor("dbg_xnT", [128, D], BF16, kind="ExternalOutput").ap()
        dbg_stt = nc.dram_tensor("dbg_stt", [128, 40 * 3], F32, kind="ExternalOutput").ap()

    ops = {e: [] for e in ["tensor", "vector", "scalar", "gpsimd", "sync"]}
    cnt = {}
    MS = {}
    sems = {}

    def W(eng, key):
        sem, val = MS[key]
        ops[eng].append(("wait", sem, val))

    def O(eng, fn, inc=None, n=1):
        ops[eng].append(("op", fn, None, None))
        if inc is not None:
            cnt[inc] += n
            return cnt[inc]
        return None

    def M(key, semname):
        MS[key] = (sems[semname], cnt[semname])

    with ExitStack() as es:
        E = es.enter_context
        w2T = E(nc.sbuf_tensor("w2T", [128, SC * TS], BF16))
        Ab = E(nc.sbuf_tensor("Ab", [128, SC * TS], BF16))
        WmT = E(nc.sbuf_tensor("WmT", [128, SC * TS], BF16))
        pb = E(nc.sbuf_tensor("pb", [128, 6 * 1024], BF16))
        fmu = [E(nc.sbuf_tensor(f"fmu{i}", [128, 1024], U8)) for i in range(2)]
        fmb = E(nc.sbuf_tensor("fmb", [128, 1024], BF16))
        w1b = E(nc.sbuf_tensor("w1b", [128, 16 * 384], BF16))
        smt = [E(nc.sbuf_tensor(f"smt{i}", [128, 512], F32)) for i in range(3)]
        xb = [E(nc.sbuf_tensor(f"xb{i}", [128, D], BF16)) for i in range(3)]
        xn = [E(nc.sbuf_tensor(f"xn{i}", [128, D], BF16)) for i in range(2)]
        xnT = [E(nc.sbuf_tensor(f"xnT{i}", [128, D], BF16)) for i in range(2)]
        xrs = [E(nc.sbuf_tensor(f"xrs{i}", [128, TS], F32)) for i in range(4)]
        gs = [E(nc.sbuf_tensor(f"gs{i}", [128, TS], F32)) for i in range(2)]
        os_ = [E(nc.sbuf_tensor(f"os{i}", [128, TS], F32)) for i in range(2)]
        stt = [E(nc.sbuf_tensor(f"stt{i}", [128, 40], F32)) for i in range(3)]
        identb = E(nc.sbuf_tensor("identb", [128, 128], BF16))
        b2row = E(nc.sbuf_tensor("b2row", [1, TS], BF16))
        b2B = E(nc.sbuf_tensor("b2B", [128, TS], F32))
        ones1 = E(nc.sbuf_tensor("ones1", [1, 128], BF16))
        pA = [E(nc.psum_tensor(f"pA{i}", [128, 512], F32)) for i in range(6)]
        pX = [E(nc.psum_tensor(f"pX{i}", [128, 1024], BF16)) for i in range(2)]
        semnames = (["pe", "dv", "ac", "id", "b2"]
                    + [f"pbp{i}" for i in range(3)]
                    + [f"fmu{i}" for i in range(2)]
                    + [f"w1p{i}" for i in range(8)]
                    + [f"smt{i}" for i in range(3)]
                    + [f"xb{i}" for i in range(3)]
                    + [f"xrs{i}" for i in range(4)]
                    + [f"st{i}" for i in range(2)])
        for nm in semnames:
            sems[nm] = E(nc.semaphore("sm_" + nm))
            cnt[nm] = 0
        s_pe, s_dv, s_ac = sems["pe"], sems["dv"], sems["ac"]

        _nums = [sems[nm].num for nm in semnames]
        assert max(_nums) - min(_nums) + 1 == len(_nums), "sems not contiguous"
        nc.gpsimd.sem_clear(range(min(_nums), max(_nums) + 1))
        nc.gpsimd.dma_start(out=identb[:, :], in_=id_in[:, :]).then_inc(sems["id"], 16)
        cnt["id"] = 16
        MS["id_ld"] = (sems["id"], 16)
        for _p in range(3):
            _u = 2 * _p
            _qh, _tc = divmod(_u, 4)
            nc.gpsimd.dma_start(
                out=pb.ap()[:, _u * 1024:(_u + 2) * 1024].rearrange(
                    "p (a c) -> p a c", a=2),
                in_=w2_in[_tc * 128:(_tc + 2) * 128,
                          _qh * 1024:(_qh + 1) * 1024].rearrange(
                    "(a p) c -> p a c", p=128)).then_inc(sems[f"pbp{_p}"], 16)
            cnt[f"pbp{_p}"] = 16
            MS[("w2_ld", _u)] = (sems[f"pbp{_p}"], 16)
            MS[("w2_ld", _u + 1)] = (sems[f"pbp{_p}"], 16)
        nc.gpsimd.dma_start(out=b2row[0:1, :],
                            in_=b2_in[0:1, :]).then_inc(sems["b2"], 16)
        cnt["b2"] = 16
        MS["b2_ld"] = (sems["b2"], 16)
        nc.all_engine_barrier()

        w2T32 = w2T.ap().rearrange("p (c t) -> p c t", t=TS)
        Ab32 = Ab.ap().rearrange("p (c t) -> p c t", t=TS)
        WmT32 = WmT.ap().rearrange("p (c t) -> p c t", t=TS)


        O("vector", lambda v: v.memset(ones1[0:1, :], 1.0).then_inc(s_dv, 1),
          inc="dv")
        M("ones", "dv")
        for _ in range(delay_dve):
            O("vector", lambda v: v.memset(xb[0][:, :], 0.0))

        fill_id = [0]

        def emit_fill(src_tile, col0):
            f = fill_id[0]
            bank = f % 2
            if f >= 2:
                W("tensor", ("evac", f - 2))
            for j in range(8):
                c = col0 + j * 128
                if j == 7:
                    O("tensor", lambda t, s=src_tile, c=c, b=bank, j=j:
                      t.transpose(pX[b][:, j * 128:(j + 1) * 128],
                                  s[:, c:c + 128], identb[:, :]).then_inc(s_pe, 1),
                      inc="pe")
                else:
                    O("tensor", lambda t, s=src_tile, c=c, b=bank, j=j:
                      t.transpose(pX[b][:, j * 128:(j + 1) * 128],
                                  s[:, c:c + 128], identb[:, :]))
            M(("fill", f), "pe")
            fill_id[0] += 1
            return f, bank

        def emit_evac(f, bank, dst_ap):
            W("vector", ("fill", f))
            src = pX[bank].ap().rearrange("p (b k) -> p b k", k=128)
            O("vector", lambda v, s=src, d=dst_ap:
              v.tensor_copy(d, s).then_inc(s_dv, 1), inc="dv")
            M(("evac", f), "dv")

        def emit_w2_unit(u):
            qh, tc = divmod(u, 4)
            if u >= 6 and u % 2 == 0:
                pslot = f"pbp{(u % 6) // 2}"
                W("gpsimd", ("fill_unit", u - 6))
                W("gpsimd", ("fill_unit", u - 5))
                O("gpsimd", lambda g, tc=tc, qh=qh, u=u, s=sems[pslot]:
                  g.dma_start(out=pb.ap()[:, (u % 6) * 1024:
                                          (u % 6) * 1024 + 2048].rearrange(
                                  "p (a c) -> p a c", a=2),
                              in_=w2_in[tc * 128:(tc + 2) * 128,
                                        qh * 1024:(qh + 1) * 1024].rearrange(
                                  "(a p) c -> p a c", p=128))
                  .then_inc(s, 16), inc=pslot, n=16)
                M(("w2_ld", u), pslot)
                M(("w2_ld", u + 1), pslot)
            if u == 0:
                W("tensor", "id_ld")
            W("tensor", ("w2_ld", u))
            f, bank = emit_fill(pb, (u % 6) * 1024)
            emit_evac(f, bank, w2T32[:, qh * 8: qh * 8 + 8, tc * 128:(tc + 1) * 128])
            M(("fill_unit", u), "pe")
            if tc == 3:
                M(("w2T_rdy", qh), "dv")

        for u in range(4):
            emit_w2_unit(u)

        W("tensor", "b2_ld")
        W("tensor", "ones")
        O("tensor", lambda t:
          t.matmul(pA[0][:, :], ones1[0:1, :], b2row[0:1, :],
                   start=True, stop=True).then_inc(s_pe, 1), inc="pe")
        M("b2B_mm", "pe")
        W("vector", "b2B_mm")
        O("vector", lambda v:
          v.tensor_copy(b2B[:, :], pA[0][:, :]).then_inc(s_dv, 1), inc="dv")
        M("b2B_ev", "dv")

        def emit_wm_dve(u):
            oc, qh = divmod(u, 4)
            l = 16 + u
            fslot = f"fmu{u % 2}"
            if u % 2 == 0:
                pslot = f"pbp{(l % 6) // 2}"
                W("gpsimd", ("fill_unit", l - 6))
                W("gpsimd", ("fill_unit", l - 5))
                O("gpsimd", lambda g, oc=oc, qh=qh, l=l, s=sems[pslot]:
                  g.dma_start(out=pb.ap()[:, (l % 6) * 1024:
                                          (l % 6) * 1024 + 2048],
                              in_=W_in[oc * 128:(oc + 1) * 128,
                                       qh * 1024:(qh + 2) * 1024])
                  .then_inc(s, 16), inc=pslot, n=16)
                M(("W_ld", u), pslot)
                M(("W_ld", u + 1), pslot)
            if u >= 2:
                W("sync", ("wm_mult", u - 2))
            O("sync", lambda sy, oc=oc, qh=qh, u=u, s=sems[fslot]:
              sy.dma_start(out=fmu[u % 2][:, :],
                          in_=fm_in[oc * 128:(oc + 1) * 128,
                                    qh * 1024:(qh + 1) * 1024])
              .then_inc(s, 16), inc=fslot, n=16)
            M(("fm_ld", u), fslot)
            W("vector", ("fm_ld", u))
            W("vector", ("W_ld", u))
            W("vector", ("fill_unit", 16 + u - 6))
            if u >= 1:
                W("vector", ("wm_mult", u - 1))
            O("vector", lambda v, u=u:
              v.tensor_copy(fmb[:, :], fmu[u % 2][:, :]).then_inc(s_dv, 1),
              inc="dv")
            M(("fm_cast", u), "dv")
            W("vector", ("fm_cast", u))
            O("vector", lambda v, u=u, l=16 + u:
              v.tensor_tensor(pb.ap()[:, (l % 6) * 1024:(l % 6) * 1024 + 1024],
                              pb.ap()[:, (l % 6) * 1024:(l % 6) * 1024 + 1024],
                              fmb[:, :], ALU.mult).then_inc(s_dv, 1), inc="dv")
            M(("wm_mult", u), "dv")

        def emit_wm_pe(u):
            oc, qh = divmod(u, 4)
            l = 16 + u
            W("tensor", ("wm_mult", u))
            f, bank = emit_fill(pb, (l % 6) * 1024)
            emit_evac(f, bank, WmT32[:, qh * 8: qh * 8 + 8, oc * 128:(oc + 1) * 128])
            M(("fill_unit", l), "pe")
            if u == 15:
                M("wmt_all_evac", "dv")

        def emit_stage_a_load(c_ld):
                xslot = f"xb{c_ld % 3}"
                if c_ld >= 3:
                    W("gpsimd", ("red", c_ld - 3))
                    W("gpsimd", ("norm", c_ld - 3))
                O("gpsimd", lambda g, c=c_ld, s=sems[xslot]:
                  g.dma_start(out=xb[c % 3][:, :],
                              in_=x_in[c * 128:(c + 1) * 128, :])
                  .then_inc(s, 16), inc=xslot, n=16)
                M(("xb_ld", c_ld), xslot)
                rslot = f"xrs{c_ld % 4}"
                if c_ld >= 4:
                    W("sync", ("epi", c_ld - 4))
                O("sync", lambda sy, c=c_ld, s=sems[rslot]:
                  sy.dma_start(out=xrs[c % 4][:, :],
                               in_=xres_in[c * 128:(c + 1) * 128, :])
                  .then_inc(s, 16), inc=rslot, n=16)
                M(("xr_ld", c_ld), rslot)

        def emit_stage_a_calc(c_ld):
                st_ = stt[c_ld % 3]

                def col(nm, s=st_):
                    return s[:, CCOL[nm]:CCOL[nm] + 1]

                W("scalar", ("xb_ld", c_ld))
                if c_ld >= 2:
                    W("scalar", ("xp", c_ld - 2))
                O("scalar", lambda s, c=c_ld, a=col("sumsq"):
                  s.activation(xn[c % 2][:, :], xb[c % 3][:, :], AF.Square,
                               accum_out=a).then_inc(s_ac, 1), inc="ac")
                M(("sq", c_ld), "ac")
                W("vector", ("xb_ld", c_ld))
                O("vector", lambda v, c=c_ld, a=col("sum"):
                  v.reduce_sum(a, xb[c % 3][:, :],
                               mybir.AxisListType.X).then_inc(s_dv, 1), inc="dv")
                M(("red", c_ld), "dv")
                W("vector", ("red", c_ld))
                W("vector", ("sq", c_ld))
                O("vector", lambda v, a=col("mean"), b=col("sum"):
                  v.tensor_scalar(a, b, 1.0 / D, None, ALU.mult).then_inc(s_dv, 1),
                  inc="dv")
                O("vector", lambda v, a=col("negmean"), b=col("sum"):
                  v.tensor_scalar(a, b, -1.0 / D, None, ALU.mult).then_inc(s_dv, 1),
                  inc="dv")
                O("vector", lambda v, a=col("ex2e"), b=col("sumsq"):
                  v.tensor_scalar(a, b, 1.0 / D, EPS, ALU.mult,
                                  ALU.add).then_inc(s_dv, 1), inc="dv")
                M(("chain", c_ld), "dv")
                W("scalar", ("chain", c_ld))
                O("scalar", lambda s, a=col("msq"), b=col("mean"):
                  s.activation(a, b, AF.Identity, scale=b).then_inc(s_ac, 1),
                  inc="ac")
                M(("msq", c_ld), "ac")
                W("vector", ("msq", c_ld))
                O("vector", lambda v, a=col("varpe"), b=col("ex2e"), d2=col("msq"):
                  v.tensor_tensor(a, b, d2, ALU.subtract).then_inc(s_dv, 1),
                  inc="dv")
                M(("varpe", c_ld), "dv")
                W("scalar", ("varpe", c_ld))
                O("scalar", lambda s, a=col("srt"), b=col("varpe"):
                  s.activation(a, b, AF.Sqrt).then_inc(s_ac, 1), inc="ac")
                M(("sqrt", c_ld), "ac")
                W("vector", ("sqrt", c_ld))
                O("vector", lambda v, a=col("rstd"), b=col("srt"):
                  v.reciprocal(a, b).then_inc(s_dv, 1), inc="dv")
                M(("rstd", c_ld), "dv")
                W("scalar", ("rstd", c_ld))
                O("scalar", lambda s, a=col("negmb"), b=col("negmean"), r=col("rstd"):
                  s.activation(a, b, AF.Identity, scale=r).then_inc(s_ac, 1),
                  inc="ac")
                M(("negmb", c_ld), "ac")
                W("scalar", ("negmb", c_ld))
                O("scalar", lambda s, c=c_ld, a=col("negmb"), r=col("rstd"):
                  s.activation(xn[c % 2][:, :], xb[c % 3][:, :], AF.Identity,
                               bias=a, scale=r).then_inc(s_ac, 1), inc="ac")
                M(("norm", c_ld), "ac")


        def emit_xp(c_xp):
                W("tensor", ("norm", c_xp))
                for q in range(4):
                    f, bank = emit_fill(xn[c_xp % 2], q * 8 * 128)
                    if c_xp >= 2:
                        W("vector", ("mm2", c_xp - 2))
                    dst = xnT[c_xp % 2].ap().rearrange(
                        "p (b k) -> p b k", k=128)[:, q * 8:(q + 1) * 8, :]
                    emit_evac(f, bank, dst)
                    MS[("evac_xp", (c_xp, q))] = MS[("evac", f)]
                M(("xp", c_xp), "pe")


        G2 = [3] * 10 + [2]
        G2B = [3 * i for i in range(11)]
        smi = 0
        for g in range(11):
            sb, sz = G2B[g], G2[g]
            bank0 = (g % 2) * 3
            for d in range(32):
                n = g * 32 + d
                if d % 2 == 0:
                    wslot = f"w1p{(n // 2) % 8}"
                    if n >= 16:
                        pg, pd = divmod(n - 16 + 1, 32)
                        W("gpsimd", ("mm1", (pg, pd)))
                    src_ap = w1_in[d * 128:(d + 2) * 128,
                                   sb * 128:(sb + sz) * 128].rearrange(
                        "(a p) c -> p a c", p=128)
                    dst_ap = w1b.ap().rearrange(
                        "p (s c) -> p s c", c=384)[:, n % 16:n % 16 + 2,
                                                   0:sz * 128]
                    O("gpsimd", lambda g_, sa=src_ap, da=dst_ap, s=sems[wslot]:
                      g_.dma_start(out=da, in_=sa).then_inc(s, 16),
                      inc=wslot, n=16)
                    M(("w1_ld", (g, d)), wslot)
                    MS[("w1_ld", (g, d + 1))] = MS[("w1_ld", (g, d))]
                if g == 0 and d % 8 == 0 and d > 0:
                    for u in range(d // 8 * 4, d // 8 * 4 + 4):
                        emit_w2_unit(u)
                W("tensor", ("w1_ld", (g, d)))
                if g == 0 and d % 8 == 0:
                    W("tensor", ("w2T_rdy", d // 8))
                for j in range(sz):
                    if d == 0 and g >= 2:
                        W("tensor", ("evA_bank", (g - 2, j)))
                    if d == 0 and g == 0 and j == 0:
                        W("tensor", "b2B_ev")
                    if j == sz - 1:
                        O("tensor", lambda t, j=j, d=d, n=n, b=bank0:
                          t.matmul(pA[b + j][:, :],
                                   w1b.ap()[:, (n % 16) * 384 + j * 128:
                                            (n % 16) * 384 + (j + 1) * 128],
                                   w2T32[:, d, :], start=(d == 0),
                                   stop=(d == 31)).then_inc(s_pe, 1), inc="pe")
                    else:
                        O("tensor", lambda t, j=j, d=d, n=n, b=bank0:
                          t.matmul(pA[b + j][:, :],
                                   w1b.ap()[:, (n % 16) * 384 + j * 128:
                                            (n % 16) * 384 + (j + 1) * 128],
                                   w2T32[:, d, :], start=(d == 0), stop=(d == 31)))
                M(("mm1", (g, d)), "pe")
            W("vector", ("mm1", (g, 31)))
            for j in range(sz):
                sslot = f"smt{smi % 3}"
                if smi >= 3:
                    W("sync", ("evA", smi - 3))
                O("sync", lambda sy, sb=sb, j=j, smi=smi, s=sems[sslot]:
                  sy.dma_start(out=smt[smi % 3][:, :],
                               in_=sm_in[(sb + j) * 128:(sb + j + 1) * 128, :])
                  .then_inc(s, 16), inc=sslot, n=16)
                M(("sm_ld", smi), sslot)
                W("vector", ("sm_ld", smi))
                O("vector", lambda v, j=j, sb=sb, smi=smi, b=bank0:
                  v.tensor_tensor(Ab32[:, sb + j, :], pA[b + j][:, :],
                                  smt[smi % 3][:, :],
                                  ALU.mult).then_inc(s_dv, 1), inc="dv")
                M(("evA", smi), "dv")
                M(("evA_bank", (g, j)), "dv")
                smi += 1
            if g == 0:
                emit_wm_dve(0)
                emit_wm_dve(1)
            if g <= 7:
                emit_wm_pe(2 * g)
                emit_wm_pe(2 * g + 1)
            if g + 1 <= 7:
                emit_wm_dve(2 * (g + 1))
                emit_wm_dve(2 * (g + 1) + 1)
            if g in (4, 5, 6, 7):
                cw, half = divmod(g - 4, 2)
                xslot = f"xb{cw % 3}"
                O("gpsimd", lambda g_, c=cw, h=half, s=sems[xslot]:
                  g_.dma_start(out=xb[c % 3][:, h * 2048:(h + 1) * 2048],
                               in_=x_in[c * 128:(c + 1) * 128,
                                        h * 2048:(h + 1) * 2048])
                  .then_inc(s, 16), inc=xslot, n=16)
                if half == 1:
                    M(("xb_ld", cw), xslot)
                    rslot = f"xrs{cw % 4}"
                    O("sync", lambda sy, c=cw, s=sems[rslot]:
                      sy.dma_start(out=xrs[c % 4][:, :],
                                   in_=xres_in[c * 128:(c + 1) * 128, :])
                      .then_inc(s, 16), inc=rslot, n=16)
                    M(("xr_ld", cw), rslot)
            if g == 8:
                emit_stage_a_calc(0)
                emit_stage_a_calc(1)
            if g == 9:
                emit_xp(0)
        M("p1_all_evac", "dv")

        P2PAIRS = [(2, 3), (4, 5), (0, 1)]

        for k in range(RC + 3):
            c_ld, c_xp, c_mm, c_ep = k, k - 1, k - 2, k - 3
            if 2 <= c_ld < RC:
                emit_stage_a_load(c_ld)
                emit_stage_a_calc(c_ld)
            if 0 <= c_mm < RC:
                pY = pA[P2PAIRS[c_mm % 3][0]]
                pF = pA[P2PAIRS[c_mm % 3][1]]
                if c_mm >= 3:
                    W("tensor", ("gelu", c_mm - 3))
                    W("tensor", ("epi", c_mm - 3))
                elif c_mm == 0:
                    W("tensor", "p1_all_evac")
                    W("tensor", "wmt_all_evac")
                elif c_mm == 1:
                    W("tensor", ("evA_bank", (9, 1)))
                    W("tensor", ("evA_bank", (9, 2)))
                else:
                    W("tensor", ("evA_bank", (10, 0)))
                    W("tensor", ("evA_bank", (10, 1)))
                do_xp = 1 <= c_xp < RC

                def emit_fill_q(q, c_xp=c_xp):
                    f, bank = emit_fill(xn[c_xp % 2], q * 8 * 128)
                    if c_xp >= 2:
                        W("vector", ("mm2", c_xp - 2))
                    dst = xnT[c_xp % 2].ap().rearrange(
                        "p (b k) -> p b k", k=128)[:, q * 8:(q + 1) * 8, :]
                    emit_evac(f, bank, dst)
                    MS[("evac_xp", (c_xp, q))] = MS[("evac", f)]

                for sc in range(SC):
                    if sc % 8 == 0:
                        W("tensor", ("evac_xp", (c_mm, sc // 8)))
                    O("tensor", lambda t, c=c_mm, sc=sc, pY=pY:
                      t.matmul(pY[:, :], xnT[c % 2][:, sc * 128:(sc + 1) * 128],
                               Ab32[:, sc, :], start=(sc == 0), stop=(sc == SC - 1)))
                    if sc == SC - 1:
                        O("tensor", lambda t, c=c_mm, sc=sc, pF=pF:
                          t.matmul(pF[:, :], xnT[c % 2][:, sc * 128:(sc + 1) * 128],
                                   WmT32[:, sc, :], start=(sc == 0),
                                   stop=True).then_inc(s_pe, 1), inc="pe")
                    else:
                        O("tensor", lambda t, c=c_mm, sc=sc, pF=pF:
                          t.matmul(pF[:, :], xnT[c % 2][:, sc * 128:(sc + 1) * 128],
                                   WmT32[:, sc, :], start=(sc == 0), stop=False))
                    if do_xp and sc == 7:
                        W("tensor", ("norm", c_xp))
                        emit_fill_q(0)
                        emit_fill_q(1)
                M(("mm2", c_mm), "pe")
                if do_xp:
                    emit_fill_q(2)
                    emit_fill_q(3)
                    M(("xp", c_xp), "pe")
            elif 1 <= c_xp < RC:
                emit_xp(c_xp)
            if 0 <= c_ep < RC:
                pY = pA[P2PAIRS[c_ep % 3][0]]
                pF = pA[P2PAIRS[c_ep % 3][1]]
                W("vector", ("mm2", c_ep))
                if c_ep >= 2:
                    W("vector", ("epi", c_ep - 2))
                O("vector", lambda v, c=c_ep, pY=pY:
                  v.tensor_tensor(gs[c % 2][:, :], pY[:, :], b2B[:, :],
                                  ALU.add).then_inc(s_dv, 1), inc="dv")
                M(("yb", c_ep), "dv")
                W("scalar", ("yb", c_ep))
                O("scalar", lambda s, c=c_ep:
                  s.activation(gs[c % 2][:, :], gs[c % 2][:, :],
                               AF.Gelu).then_inc(s_ac, 1), inc="ac")
                M(("gelu", c_ep), "ac")
                W("vector", ("gelu", c_ep))
                W("vector", ("xr_ld", c_ep))
                if c_ep >= 2:
                    W("vector", ("store", c_ep - 2))
                O("vector", lambda v, c=c_ep:
                  v.tensor_tensor(os_[c % 2][:, :], gs[c % 2][:, :],
                                  xrs[c % 4][:, :], ALU.add).then_inc(s_dv, 1),
                  inc="dv")
                M(("epi_a", c_ep), "dv")
                W("vector", ("epi_a", c_ep))
                O("vector", lambda v, c=c_ep, pF=pF:
                  v.tensor_tensor(os_[c % 2][:, :], os_[c % 2][:, :],
                                  pF[:, :], ALU.add).then_inc(s_dv, 1), inc="dv")
                M(("epi", c_ep), "dv")
                stslot = f"st{c_ep % 2}"
                W("sync", ("epi", c_ep))
                O("sync", lambda sy, c=c_ep, s=sems[stslot]:
                  sy.dma_start(out=out_t[c * 128:(c + 1) * 128, :],
                               in_=os_[c % 2][:, :]).then_inc(s, 16),
                  inc=stslot, n=16)
                M(("store", c_ep), stslot)

        ops["sync"].append(("wait", sems["st0"], cnt["st0"]))
        ops["sync"].append(("wait", sems["st1"], cnt["st1"]))

        if debug:
            def _dump(sy):
                sy.dma_start(out=dbg_w2T[:, :], in_=w2T.ap()[:, :]).then_inc(sems["st0"], 16)
                sy.dma_start(out=dbg_Ab[:, :], in_=Ab.ap()[:, :]).then_inc(sems["st0"], 16)
                sy.dma_start(out=dbg_WmT[:, :], in_=WmT.ap()[:, :]).then_inc(sems["st0"], 16)
                sy.dma_start(out=dbg_xn[:, :], in_=xn[(RC - 1) % 2].ap()[:, :]).then_inc(sems["st0"], 16)
                sy.dma_start(out=dbg_xnT[:, :], in_=xnT[(RC - 1) % 2].ap()[:, :]).then_inc(sems["st0"], 16)
                for i in range(3):
                    sy.dma_start(out=dbg_stt[:, i * 40:(i + 1) * 40],
                                 in_=stt[i].ap()[:, :]).then_inc(sems["st0"], 16)
            O("sync", _dump, inc="st0", n=16 * 8)
            ops["sync"].append(("wait", sems["st0"], cnt["st0"]))

        with nc.Block() as block:

            @block.tensor
            def _(t):
                for item in ops["tensor"]:
                    if item[0] == "wait":
                        t.wait_ge(item[1], item[2])
                    else:
                        item[1](t)

            @block.vector
            def _(v):
                for item in ops["vector"]:
                    if item[0] == "wait":
                        v.wait_ge(item[1], item[2])
                    else:
                        item[1](v)

            @block.scalar
            def _(s):
                for item in ops["scalar"]:
                    if item[0] == "wait":
                        s.wait_ge(item[1], item[2])
                    else:
                        item[1](s)

            @block.gpsimd
            def _(g):
                for item in ops["gpsimd"]:
                    if item[0] == "wait":
                        g.wait_ge(item[1], item[2])
                    else:
                        item[1](g)

            @block.sync
            def _(sy):
                for item in ops["sync"]:
                    if item[0] == "wait":
                        sy.wait_ge(item[1], item[2])
                    else:
                        item[1](sy)

    return nc


def kernel(L0_attn_x, L0_attn_w1, L0_attn_w2, L0_attn_b2, L0_attn_sparse_mask,
           L0_ffn_W, L0_ffn_mask):
    x = np.ascontiguousarray(np.asarray(L0_attn_x, dtype=np.float32)).reshape(NT, D)
    w1 = np.ascontiguousarray(np.asarray(L0_attn_w1, dtype=np.float32))
    w2 = np.ascontiguousarray(np.asarray(L0_attn_w2, dtype=np.float32))
    b2 = np.ascontiguousarray(np.asarray(L0_attn_b2, dtype=np.float32))
    sm = np.ascontiguousarray(np.asarray(L0_attn_sparse_mask, dtype=np.float32))
    Wf = np.ascontiguousarray(np.asarray(L0_ffn_W, dtype=np.float32))
    fm = np.ascontiguousarray(np.asarray(L0_ffn_mask)).astype(np.uint8)
    ident = np.eye(128, dtype=np.float32)

    nc = build_program()
    in_maps = []
    for c in range(8):
        cs = c * TS
        in_maps.append({
            "x_in": x,
            "xres_in": np.ascontiguousarray(x[:, cs:cs + TS]),
            "w1_in": w1,
            "w2_in": np.ascontiguousarray(w2[cs:cs + TS, :]),
            "b2_in": np.ascontiguousarray(b2[cs:cs + TS]).reshape(1, TS),
            "sm_in": np.ascontiguousarray(sm[:, cs:cs + TS]),
            "W_in": np.ascontiguousarray(Wf[cs:cs + TS, :]),
            "fm_in": np.ascontiguousarray(fm[cs:cs + TS, :]),
            "id_in": ident,
        })
    res = run_bass_kernel_spmd(nc, in_maps, core_ids=list(range(8)))
    out = np.concatenate([np.asarray(r["out_t"], dtype=np.float32)
                          for r in res.results], axis=1)
    return out.reshape(2, 4096, 4096)
